# version 20
# baseline (speedup 1.0000x reference)
import numpy as np
import ml_dtypes
import concourse.bass as bass
import concourse.mybir as mybir
from concourse.bass_utils import run_bass_kernel_spmd

F32 = mybir.dt.float32
BF16 = mybir.dt.bfloat16
I32 = mybir.dt.int32
U8 = mybir.dt.uint8
AF = mybir.ActivationFunctionType
ALU = mybir.AluOpType
AX = mybir.AxisListType
BF = ml_dtypes.bfloat16

P = 128
D = 4096
T = 1024
KC = 32
NG = 16
NCORES = 8
EPS = 1e-5

_CACHE = {}
_DEBUG = False


def _rfo_matrix_T():
    if "FT" in _CACHE:
        return _CACHE["FT"]
    n = D
    eye = np.eye(n, dtype=np.float64)
    max_f = (n - 1) // 2
    spec = np.zeros((n, n // 2 + 1), dtype=np.complex128)
    spec[:, 0] = eye[:, 0] * np.sqrt(float(n))
    spec[:, 1:1 + max_f] = (eye[:, 1:1 + 2 * max_f:2] - 1j * eye[:, 2:2 + 2 * max_f:2]) * np.sqrt(n / 2.0)
    spec[:, n // 2] = eye[:, -1] * np.sqrt(float(n))
    ft = np.fft.irfft(spec, n=n, axis=-1)
    ft = np.ascontiguousarray(ft.astype(BF))
    _CACHE["FT"] = ft
    return ft


class _Plan:
    def __init__(self, ctr):
        self.streams = {e: [] for e in ("sync", "gpsimd", "vector", "scalar", "tensor")}
        self.ctr = ctr

    def val(self, sem):
        return self.ctr.get(sem, 0)

    def op(self, eng, fn, waits=(), inc=None, inc_n=1):
        if inc is not None:
            self.ctr[inc] = self.ctr.get(inc, 0) + inc_n
        self.streams[eng].append((list(waits), fn, inc, inc_n))
        return self.ctr.get(inc, 0) if inc is not None else None

    def wait(self, eng, waits):
        self.streams[eng].append((list(waits), None, None, 0))


def _emit_block(nc, plan, sems, name):
    with nc.Block(no_gpsimd_drain=True) as blk:
        for eng_name in ("sync", "gpsimd", "vector", "scalar", "tensor"):
            ops = plan.streams[eng_name]
            if not ops:
                continue

            def mk(ops):
                def body(eng):
                    for waits, fn, inc, inc_n in ops:
                        for (s, v) in waits:
                            if v > 0:
                                eng.wait_ge(sems[s], v)
                        if fn is None:
                            continue
                        ins = fn(eng)
                        if inc is not None:
                            ins.then_inc(sems[inc], inc_n)
                return body

            getattr(blk, eng_name)(mk(ops))


def _build_nc():
    if "nc" in _CACHE:
        return _CACHE["nc"]
    from contextlib import ExitStack

    nc = bass.Bass("TRN2", target_bir_lowering=False, debug=False)

    d_xT = nc.dram_tensor("xT", [P, KC * T], F32, kind="ExternalInput").ap()
    d_wT = nc.dram_tensor("wT", [P, NG * 8192], F32, kind="ExternalInput").ap()
    d_maskS = nc.dram_tensor("maskS", [P, 4 * D], I32, kind="ExternalInput").ap()
    d_wbT = nc.dram_tensor("wbT", [P, NG * 8192], F32, kind="ExternalInput").ap()
    d_FTS = nc.dram_tensor("FTS", [P, 32 * 512], BF16, kind="ExternalInput").ap()
    d_b2d = nc.dram_tensor("b2d", [P, KC], F32, kind="ExternalInput").ap()
    d_encT = nc.dram_tensor("encT", [P, 1280], F32, kind="ExternalInput").ap()
    d_c0T = nc.dram_tensor("c0T", [P, 3072], F32, kind="ExternalInput").ap()
    d_c1T = nc.dram_tensor("c1T", [P, 768], F32, kind="ExternalInput").ap()
    d_m0T = nc.dram_tensor("m0T", [P, 3072], U8, kind="ExternalInput").ap()
    d_m1T = nc.dram_tensor("m1T", [P, 768], U8, kind="ExternalInput").ap()
    d_g32 = nc.dram_tensor("g32", [P, P], BF16, kind="ExternalInput").ap()
    d_out = nc.dram_tensor("out", [D, T], F32, kind="ExternalOutput").ap()
    cc_pin = nc.dram_tensor("cc_pin", [512, P], BF16, kind="Internal").ap()
    cc_pout = nc.dram_tensor("cc_pout", [D, P], BF16, kind="Internal").ap()
    cc_hin = nc.dram_tensor("cc_hin", [40, 512], BF16, kind="Internal").ap()
    cc_hout = nc.dram_tensor("cc_hout", [320, 512], BF16, kind="Internal").ap()

    dbg = {}
    if _DEBUG:
        for nm, shape in [("dbg_patk", [P, KC * P]), ("dbg_HT", [P, D]),
                          ("dbg_x2", [P, KC * T]), ("dbg_x2l", [P, KC * T]),
                          ("dbg_e", [P, T]), ("dbg_w01", [P, 1280])]:
            dbg[nm] = nc.dram_tensor(nm, shape, BF16, kind="ExternalOutput").ap()

    st = ExitStack()
    nc._keepalive = st

    def sb(name, shape, dt):
        return st.enter_context(nc.sbuf_tensor(name, shape, dt))

    xa = sb("xa", [P, KC * T], BF16)
    xb = sb("xb", [P, KC * T], BF16)
    wg = [sb("wg0", [P, 8192], BF16), sb("wg1", [P, 8192], BF16)]
    wm = [sb("wm0", [P, 256], BF16), sb("wm1", [P, 256], BF16)]
    patk = sb("patk", [P, KC * P], BF16)
    HT = sb("HT", [P, D], BF16)
    e_pad = sb("e_pad", [P, T], BF16)
    b2d = sb("b2d_sb", [P, KC], F32)
    ones128 = sb("ones128", [P, 1], BF16)
    tmpbf = [sb("tmpbf0", [P, T], BF16), sb("tmpbf1", [P, T], BF16)]
    outb = [sb("outb0", [P, T], F32), sb("outb1", [P, T], F32)]
    rows = sb("rows", [1, 2180], F32)
    sqrb = sb("sqrb", [P, 4096], BF16)
    eps_c = rows[0:1, 2176:2177]

    g32 = patk[:, 3840:3968]
    m0st = wg[0][:, 0:3072]
    m1st = wg[0][:, 3072:3840]
    W01 = wg[0][:, 4096:5376]
    mhalfX = [xb[:, 0:2048], xb[:, 2048:4096]]
    ftile = [wg[0][:, 5376:7424], wg[1][:, 0:2048]]
    red = [wg[0][:, 7424 + 128 * i:7552 + 128 * i] for i in range(4)]
    patk_loc = wg[1][:, 2048:2560]
    c0st = wg[1][:].bitcast(F32)[:, 0:3072]
    c1st = wg[1][:].bitcast(F32)[:, 3072:3840]
    sq = [sqrb[:, 0:1024], sqrb[:, 1024:2048]]
    rB_bf = sqrb[:, 2048:3072]
    mrB_bf = sqrb[:, 3072:4096]
    encvB = wm[0][:, 0:240]
    encvC = wm[1][:, 0:40]

    S0 = rows[0:1, 0:1024]
    ones_row = rows[0:1, 1024:1152]
    Trow = rows[0:1, 1152:2176]
    S1 = outb[1][0:1, 0:1024]
    S2 = outb[0][0:1, 0:1024]

    def xav(c):
        return xa[:, c * T:(c + 1) * T]

    def xbv(c):
        return xb[:, c * T:(c + 1) * T]

    def encv(c):
        if c < 25:
            return patk[:, c * 40:(c + 1) * 40]
        if c < 31:
            return encvB[:, (c - 25) * 40:(c - 24) * 40]
        return encvC

    sems = {}
    for name in ("S", "M", "X1", "X2", "X3", "X4", "Ka", "Kb", "P1", "Ca", "Cb", "H1",
                 "E", "Wa", "Wb", "Wc", "Fa", "Fb", "Oa", "Ob", "Pe", "V", "A"):
        sems[name] = st.enter_context(nc.semaphore("sem_" + name))
    ctr = {}

    nc.const_aps.register(F32, EPS, eps_c)

    def dump(name, src_ap):
        if not _DEBUG:
            return
        pl = _Plan(ctr)
        pl.op("sync", lambda e: e.dma_start(dbg[name], src_ap), inc="S", inc_n=16)
        pl.wait("sync", [("S", pl.val("S"))])
        _emit_block(nc, pl, sems, "dump_" + name)


    with ExitStack() as pst:
        ps4 = [[pst.enter_context(nc.psum_tensor(f"ps{i}{j}", [P, T], F32))
                for j in range(2)] for i in range(2)]
        ps_pat = [ps4[1][1][:, 0:128], ps4[1][1][:, 512:640]]
        ps_H = ps4[1][0][0:40, 512:1024]
        ps_sum = {1: ps4[0][0][0:1, 0:1024], 2: ps4[0][0][0:1, 0:1024]}
        ps_sq = {1: ps4[0][1][0:1, 0:1024], 2: ps4[0][1][0:1, 0:1024]}
        ps_bc = {1: ps4[1][0][:, 0:512], 2: ps4[0][0][:, 0:512]}
        ps_e = ps4[1][0][0:40, 0:1024]

        def pair3(gi):
            return gi % 2

        def plan_main(pl, rs):
            tb = {}

            def R(*keys):
                d = rs
                if d is None:
                    return 0
                for k in keys:
                    d = d[k]
                return d

            pl.op("sync", lambda e: e.dma_start(b2d[:], d_b2d), inc="S", inc_n=16)
            pl.op("sync", lambda e: e.dma_start(g32, d_g32), inc="S", inc_n=16)
            pl.op("sync", lambda e: e.dma_start(c0st, d_c0T), inc="S", inc_n=16)
            pl.op("sync", lambda e: e.dma_start(c1st, d_c1T), inc="S", inc_n=16)
            s_small = pl.val("S")
            pl.op("gpsimd", lambda e: e.dma_start(m0st, d_m0T), inc="M", inc_n=16)
            pl.op("gpsimd", lambda e: e.dma_start(m1st, d_m1T), inc="M", inc_n=16)
            gm = {}
            for i in range(8):
                cl, hf = divmod(i, 2)
                waits = [("V", R("vr", i - 2))] if i >= 2 else []
                sk = "Ka" if i % 2 == 0 else "Kb"
                src = d_maskS[:, cl * D + hf * 2048:cl * D + (hf + 1) * 2048]
                pl.op("gpsimd", lambda e, s_=src, d_=mhalfX[i % 2]: e.dma_start(d_, s_),
                      waits=waits, inc=sk, inc_n=16)
                gm[i] = (sk, pl.val(sk))
            tb["gm"] = gm
            for i in range(4):
                src_ = d_xT[:, i * 8 * T:(i + 1) * 8 * T]
                dst_ = xa[:, i * 8 * T:(i + 1) * 8 * T]
                sx = ("X1", "X2", "X3", "X4")[i]
                pl.op("gpsimd", lambda e, s_=src_, d_=dst_: e.dma_start(d_, s_), inc=sx, inc_n=16)
            pl.op("vector", lambda e: e.memset(eps_c, EPS))
            pl.op("vector", lambda e: e.memset(ones_row, 1.0))
            pl.op("vector", lambda e: e.memset(HT[:], 0.0))
            pl.op("vector", lambda e: e.memset(e_pad[:], 0.0))
            pl.op("vector", lambda e: e.memset(ones128[:], 1.0), inc="V")
            v_memset = pl.val("V")
            tb["v_memset"] = v_memset
            prod0 = patk[:, 0:3072]
            prod1 = patk[:, 3072:3840]
            p0v = prod0.rearrange("p (c e r) -> p c e r", e=3, r=32)
            p1v = prod1.rearrange("p (c e r) -> p c e r", e=3, r=8)
            s1v = tmpbf[0][:, 0:1024].rearrange("p (c r) -> p c r", r=32)
            s2v = tmpbf[1][:, 0:256].rearrange("p (c r) -> p c r", r=8)
            w01v = W01.rearrange("p (c r) -> p c r", r=40)
            pl.op("vector", lambda e: e.tensor_tensor(prod0, c0st, m0st, ALU.mult),
                  waits=[("S", s_small), ("M", 32), ("V", pl.val("V"))], inc="V")
            pl.op("vector", lambda e: e.tensor_tensor(s1v, p0v[:, :, 0:1, :].squeeze(2), p0v[:, :, 1:2, :].squeeze(2), ALU.add),
                  waits=[("V", pl.val("V"))], inc="V")
            pl.op("vector", lambda e: e.tensor_tensor(w01v[:, :, 0:32], s1v, p0v[:, :, 2:3, :].squeeze(2), ALU.add),
                  waits=[("V", pl.val("V"))], inc="V")
            pl.op("vector", lambda e: e.tensor_tensor(prod1, c1st, m1st, ALU.mult),
                  waits=[("V", pl.val("V"))], inc="V")
            pl.op("vector", lambda e: e.tensor_tensor(s2v, p1v[:, :, 0:1, :].squeeze(2), p1v[:, :, 1:2, :].squeeze(2), ALU.add),
                  waits=[("V", pl.val("V"))], inc="V")
            pl.op("vector", lambda e: e.tensor_tensor(w01v[:, :, 32:40], s2v, p1v[:, :, 2:3, :].squeeze(2), ALU.add),
                  waits=[("V", pl.val("V"))], inc="V")
            tb["v_w01"] = pl.val("V")

            pl.op("gpsimd", lambda e: e.dma_start(
                cc_pin.rearrange("(cl p) nb -> p cl nb", p=P),
                patk_loc.rearrange("p (cl nb) -> p cl nb", nb=P)),
                waits=[("V", R("vm", 3))], inc="P1", inc_n=16)
            pl.op("gpsimd", lambda e: e.collective_compute(
                "AllGather", ALU.bypass, replica_groups=[list(range(NCORES))],
                ins=[cc_pin.opt()], outs=[cc_pout.opt()]),
                waits=[("P1", 16)], inc="Ca")
            pl.op("gpsimd", lambda e: e.dma_start(
                patk[:].rearrange("p (c nb) -> p c nb", nb=P),
                cc_pout.rearrange("(c p) nb -> p c nb", p=P)),
                waits=[("Ca", 1)], inc="P1", inc_n=16)
            tb["g_patk"] = ("P1", 32)
            gdma3 = {}
            g0keys = {}
            g0waits = [
                [("V", R("v_w01"))],
                [("V", R("v_w01"))],
                [("Pe", R("ph", 7)), ("V", R("vr", 6))],
                [("Pe", R("ph", 7)), ("Pe", R("pg", 3)), ("V", R("vr", 7))],
            ]
            for q in range(4):
                sk = ("Wa", "Wc", "Ka", "Kb")[q]
                src = d_wT[:, q * 2048:(q + 1) * 2048]
                dst = wg[0][:, q * 2048:(q + 1) * 2048]
                pl.op("gpsimd", lambda e, s_=src, d_=dst: e.dma_start(d_, s_),
                      waits=g0waits[q], inc=sk, inc_n=16)
                g0keys[q] = (sk, pl.val(sk))
            gdma3[0] = g0keys[0]
            tb["g0keys"] = g0keys
            pl.op("gpsimd", lambda e: e.dma_start(wg[1][:, 0:4096], d_wT[:, 8192:12288]),
                waits=[("Pe", R("ph", 7)), ("V", R("v_w01")), ("P1", 16)], inc="Wb", inc_n=16)
            gdma3[1] = ("Wb", pl.val("Wb"))
            pl.op("gpsimd", lambda e: e.dma_start(wg[1][:, 4096:8192], d_wT[:, 12288:16384]),
                waits=[("V", R("vmult", (1, 0)))], inc="Wb", inc_n=16)
            tb["g1b"] = ("Wb", pl.val("Wb"))
            g3b = {}
            for gi in range(2, NG):
                sw = "Wa" if gi % 2 == 0 else "Wb"
                src = d_wT[:, gi * 8192:(gi + 1) * 8192]
                pl.op("gpsimd", lambda e, s_=src[:, 0:4096], d_=wg[gi % 2][:, 0:4096]:
                      e.dma_start(d_, s_),
                      waits=[("V", R("vmult", (gi - 2, KC - 1)))], inc=sw, inc_n=16)
                gdma3[gi] = (sw, pl.val(sw))
                pl.op("gpsimd", lambda e, s_=src[:, 4096:8192], d_=wg[gi % 2][:, 4096:8192]:
                      e.dma_start(d_, s_),
                      waits=[("V", R("vmult", (gi, 0)))], inc=sw, inc_n=16)
                g3b[gi] = (sw, pl.val(sw))
                if gi == 2:
                    pl.op("gpsimd", lambda e: e.dma_start(cc_hin, HT[0:40, 0:512]),
                          waits=[("A", R("ahev", 0))], inc="H1", inc_n=16)
                    pl.op("gpsimd", lambda e: e.collective_compute(
                        "AllGather", ALU.bypass, replica_groups=[list(range(NCORES))],
                        ins=[cc_hin.opt()], outs=[cc_hout.opt()]),
                        waits=[("H1", 16)], inc="Cb")
                    pl.op("gpsimd", lambda e: e.dma_start(
                        HT[0:40, :].rearrange("p (r u) -> p r u", u=512),
                        cc_hout.rearrange("(r p) u -> p r u", p=40)),
                        waits=[("Cb", 1)], inc="H1", inc_n=16)
                    tb["g_HT"] = ("H1", 32)
            tb["gdma3"] = gdma3
            tb["g3b"] = g3b
            gdma6 = {}
            for gi in range(NG):
                if gi == 0:
                    waits = [("V", R("vmult", (14, KC - 1)))]
                elif gi == 1:
                    waits = [("V", R("vmult", (15, KC - 1)))]
                else:
                    waits = [("Pe", R("pstop6", gi - 2))]
                sw = "Wa" if gi % 2 == 0 else "Wb"
                src = d_wbT[:, gi * 8192:(gi + 1) * 8192]
                pl.op("gpsimd", lambda e, s_=src, d_=wg[gi % 2][:]: e.dma_start(d_, s_),
                      waits=waits, inc=sw, inc_n=16)
                gdma6[gi] = (sw, pl.val(sw))
                if gi == 1:
                    pl.op("gpsimd", lambda e: e.dma_start(patk[:, 0:1000], d_encT[:, 0:1000]),
                        waits=[("V", R("vmult", (NG - 1, 7)))], inc="E", inc_n=16)
                    pl.op("gpsimd", lambda e: e.dma_start(encvB, d_encT[:, 1000:1240]),
                        waits=[("Pe", R("pmm", (NG - 1, KC - 1)))], inc="E", inc_n=16)
                    pl.op("gpsimd", lambda e: e.dma_start(encvC, d_encT[:, 1240:1280]),
                        inc="E", inc_n=16)
                    tb["g_enc"] = ("E", 48)
            tb["gdma6"] = gdma6

            sf = {}
            for j in range(8):
                waits = [("Pe", R("ph", j - 2))] if j >= 2 else [("V", R("v_w01"))]
                sfx = "Fa" if j % 2 == 0 else "Fb"
                src = d_FTS[:, j * 2048:(j + 1) * 2048]
                pl.op("sync", lambda e, s_=src, d_=ftile[j % 2]: e.dma_start(d_, s_),
                      waits=waits, inc=sfx, inc_n=16)
                sf[j] = (sfx, pl.val(sfx))
            tb["sf"] = sf
            odma = {}
            for gi in range(NG):
                for ncl in range(2):
                    nc_ = 2 * gi + ncl
                    so = "Oa" if ncl == 0 else "Ob"
                    dst = d_out[nc_ * P:(nc_ + 1) * P, :]
                    pl.op("sync", lambda e, d_=dst, ncl=ncl: e.dma_start(d_, outb[ncl][:]),
                          waits=[("V", R("vev6", (gi, ncl)))], inc=so, inc_n=16)
                    odma[(gi, ncl)] = (so, pl.val(so))
            tb["odma"] = odma
            pl.wait("sync", [("Oa", pl.val("Oa")), ("Ob", pl.val("Ob"))])

            asq1 = {}
            for c in range(KC):
                waits = [(("X1", "X2", "X3", "X4")[c // 8], 16)]
                if c >= 2:
                    waits.append(("Pe", R("pst1", c - 2)))
                pl.op("scalar", lambda e, c=c: e.activation(sq[c % 2], xav(c), AF.Square),
                      waits=waits, inc="A")
                asq1[c] = pl.val("A")
            tb["asq1"] = asq1
            pl.op("scalar", lambda e: e.activation(S2, S1, AF.Sqrt, bias=EPS),
                  waits=[("V", R("vfin1", 0))], inc="A")
            tb["asrt1"] = pl.val("A")
            abc1 = {}
            for i, dst in enumerate([rB_bf[:, 0:512], rB_bf[:, 512:1024],
                                     mrB_bf[:, 0:512], mrB_bf[:, 512:1024]]):
                pl.op("scalar", lambda e, d_=dst: e.activation(d_, ps_bc[1], AF.Copy),
                      waits=[("Pe", R("pbc1", i))], inc="A")
                abc1[i] = pl.val("A")
            tb["abc1"] = abc1
            pl.op("scalar", lambda e: e.activation(HT[0:40, 0:512], ps_H, AF.Copy, scale=0.5),
                  waits=[("Pe", R("ph", 7)), ("V", R("v_memset"))], inc="A")
            tb["ahev"] = {0: pl.val("A")}
            acp, asq2 = {}, {}
            for gi in range(NG):
                for ncl in range(2):
                    nc_ = 2 * gi + ncl
                    pl.op("scalar", lambda e, ncl=ncl, nc_=nc_:
                          e.activation(xav(nc_), tmpbf[ncl][:], AF.Copy),
                          waits=[("V", R("vev3", (gi, ncl)))], inc="A")
                    acp[(gi, ncl)] = pl.val("A")
            for c in range(KC):
                waits = [("A", acp[(c // 2, c % 2)])]
                if c >= 2:
                    waits.append(("Pe", R("pst2", c - 2)))
                pl.op("scalar", lambda e, c=c: e.activation(sq[c % 2], xav(c), AF.Square),
                      waits=waits, inc="A")
                asq2[c] = pl.val("A")
            tb["acp"] = acp
            tb["asq2"] = asq2
            pl.op("scalar", lambda e: e.activation(S2, S1, AF.Sqrt, bias=EPS),
                  waits=[("V", R("vfin2", 0))], inc="A")
            tb["asrt2"] = pl.val("A")
            abc2 = {}
            for i, dst in enumerate([rB_bf[:, 0:512], rB_bf[:, 512:1024],
                                     mrB_bf[:, 0:512], mrB_bf[:, 512:1024]]):
                pl.op("scalar", lambda e, d_=dst: e.activation(d_, ps_bc[2], AF.Copy),
                      waits=[("Pe", R("pbc2", i))], inc="A")
                abc2[i] = pl.val("A")
            tb["abc2"] = abc2
            pl.op("scalar", lambda e: e.activation(e_pad[0:40, :], ps_e, AF.Copy),
                  waits=[("Pe", R("p_e", 0)), ("V", R("v_memset"))], inc="A")
            tb["a_eevac"] = {0: pl.val("A")}

            vr, vm = {}, {}

            def emit_reduce(i):
                cl, hf = divmod(i, 2)
                waits = [gm[i]]
                src = mhalfX[i % 2].rearrange("p (a b) -> p a b", b=32)
                dst = red[cl][:, hf * 64:(hf + 1) * 64]

                def do_reduce(e, s_=src, d_=dst):
                    with nc.allow_low_precision(reason="exact small-int blocksum"):
                        return e.tensor_reduce(d_, s_, AX.X, ALU.add)

                pl.op("vector", do_reduce, waits=waits + [("V", pl.val("V"))], inc="V")
                vr[i] = pl.val("V")

            def emit_min(cl):
                dst = patk_loc[:, cl * P:(cl + 1) * P]
                pl.op("vector", lambda e, s_=ps_pat[cl % 2], d_=dst: e.tensor_scalar_min(d_, s_, 1.0),
                      waits=[("Pe", R("pg", cl)), ("V", pl.val("V"))], inc="V")
                vm[cl] = pl.val("V")

            for it in [0, 1, 2, 3, 4, 5, 6, 7, (0,), (1,), (2,), (3,)]:
                if isinstance(it, tuple):
                    emit_min(it[0])
                else:
                    emit_reduce(it)
            tb["vr"] = vr
            tb["vm"] = vm

            def emit_finish(tag, psum_row, psq_row, pst_last, asrt_count):
                pl.op("vector", lambda e: e.tensor_scalar_mul(S0, psum_row, 1.0 / D),
                      waits=[("Pe", pst_last), ("V", pl.val("V"))], inc="V")
                pl.op("vector", lambda e: e.tensor_tensor(S2, S0, S0, ALU.mult),
                      waits=[("V", pl.val("V"))], inc="V")
                pl.op("vector", lambda e: e.scalar_tensor_tensor(S1, psq_row, 1.0 / D, S2,
                                                                ALU.mult, ALU.subtract),
                      waits=[("V", pl.val("V"))], inc="V")
                tb["vfin" + tag] = {0: pl.val("V")}
                pl.op("vector", lambda e: e.reciprocal(S1, S2), waits=[("A", asrt_count)], inc="V")
                pl.op("vector", lambda e: e.tensor_tensor(S2, S0, S1, ALU.mult),
                      waits=[("V", pl.val("V"))], inc="V")
                tb["vfin" + tag][1] = pl.val("V")

            def norm_pair(c, extra):
                pl.op("vector", lambda e, c=c: e.tensor_tensor(tmpbf[0][:], xav(c), rB_bf, ALU.mult),
                      waits=list(extra) + [("V", pl.val("V"))], inc="V")
                pl.op("vector", lambda e, c=c: e.tensor_tensor(xbv(c), tmpbf[0][:], mrB_bf, ALU.subtract),
                      waits=[("V", pl.val("V"))], inc="V")
                return pl.val("V")

            emit_finish("1", ps_sum[1], ps_sq[1], R("pst1", KC - 1), tb["asrt1"])
            vmult, vnorm1 = {}, {}

            def emit_mult(gi, c):
                waits = []
                if gi == 0:
                    if c % 8 == 0:
                        waits.append(tb["g0keys"][c // 8])
                    if c == 0:
                        waits.append(tb["g_patk"])
                elif c == 0:
                    waits.append(gdma3[gi])
                elif c == 16:
                    if gi == 1:
                        waits.append(tb["g1b"])
                    else:
                        waits.append(tb["g3b"][gi])
                key = (gi, c - 2) if c >= 2 else ((gi - 1, KC - 2 + c) if gi >= 1 else None)
                if key is not None:
                    waits.append(("Pe", R("pmm", key)))
                w_in = wg[gi % 2][:, c * 256:(c + 1) * 256].rearrange("p (a b) -> p a b", b=32)
                p_in = patk[:, c * P + gi * 8:c * P + gi * 8 + 8].unsqueeze(2).broadcast_to([P, 8, 32])
                w_out = wm[c % 2][:].rearrange("p (a b) -> p a b", b=32)
                pl.op("vector", lambda e, a=w_out, x=w_in, y=p_in: e.tensor_tensor(a, x, y, ALU.mult),
                      waits=waits, inc="V")
                vmult[(gi, c)] = pl.val("V")

            for c in range(KC):
                vnorm1[c] = norm_pair(c, [("A", abc1[3])] if c == 0 else [])
                emit_mult(0, c)
            tb["vnorm1"] = vnorm1

            vev3 = {}

            def emit_evac3(gi):
                for ncl in range(2):
                    waits = [("Pe", R("pstop3", gi))]
                    if gi >= 1:
                        waits.append(("A", R("acp", (gi - 1, ncl))))
                    nc_ = 2 * gi + ncl
                    pl.op("vector", lambda e, gi=gi, ncl=ncl, nc_=nc_:
                          e.tensor_tensor(tmpbf[ncl][:], ps4[pair3(gi)][ncl][:], xav(nc_), ALU.add),
                          waits=waits, inc="V")
                    vev3[(gi, ncl)] = pl.val("V")

            for gi in range(1, NG):
                for c in range(KC):
                    emit_mult(gi, c)
                emit_evac3(gi - 1)
            emit_evac3(NG - 1)
            tb["vmult"] = vmult
            tb["vev3"] = vev3

            emit_finish("2", ps_sum[2], ps_sq[2], R("pst2", KC - 1), tb["asrt2"])
            vnorm2 = {}
            for c in range(KC):
                extra = []
                if c == 0:
                    extra = [("A", abc2[3]), ("Pe", R("pstop3", NG - 1))]
                vnorm2[c] = norm_pair(c, extra)
            tb["vnorm2"] = vnorm2
            vev6 = {}
            for gi in range(NG):
                for ncl in range(2):
                    nc_ = 2 * gi + ncl
                    waits = [("Pe", R("pstop6", gi))]
                    if gi >= 1:
                        waits.append(("Oa" if ncl == 0 else "Ob", 16 * gi))
                    pl.op("vector", lambda e, gi=gi, ncl=ncl, nc_=nc_:
                          e.scalar_tensor_tensor(outb[ncl][:], ps4[gi % 2][ncl][:],
                                                 b2d[:, nc_:nc_ + 1], xav(nc_),
                                                 ALU.add, ALU.add),
                          waits=waits + [("V", pl.val("V"))], inc="V")
                    vev6[(gi, ncl)] = pl.val("V")
            tb["vev6"] = vev6

            pg, ph = {}, {}

            def emit_pg(cl):
                waits = [("V", vr[2 * cl + 1])]
                if cl >= 2:
                    waits.append(("V", R("vm", cl - 2)))
                pl.op("tensor", lambda e, cl=cl: e.matmul(ps_pat[cl % 2], g32, red[cl],
                                                          start=True, stop=True, skip_group_check=True),
                      waits=waits, inc="Pe")
                pg[cl] = pl.val("Pe")

            def emit_ph(j):
                for cl4 in range(4):
                    co = 4 * j + cl4
                    lhsT = W01[:, co * 40:(co + 1) * 40]
                    rhs = ftile[j % 2][:, cl4 * 512:(cl4 + 1) * 512]
                    pl.op("tensor", lambda e, l_=lhsT, r_=rhs, co=co:
                          e.matmul(ps_H, l_, r_, start=(co == 0), stop=(co == 31),
                                   skip_group_check=True),
                          waits=[sf[j]] if cl4 == 0 else [],
                          inc="Pe" if cl4 == 3 else None)
                ph[j] = pl.val("Pe")

            def emit_stats(tag, asq, psum_row, psq_row, c_range, first_extra=()):
                pst_ = tb.setdefault("pst" + tag, {})
                for c in c_range:
                    for h in range(2):
                        rhs = xav(c)[:, h * 512:(h + 1) * 512]
                        dst = psum_row[0:1, h * 512:(h + 1) * 512]
                        waits = [("A", asq[c])] if h == 0 else []
                        if c == 0 and h == 0:
                            waits = waits + list(first_extra)
                        pl.op("tensor", lambda e, d_=dst, r_=rhs, c=c:
                              e.matmul(d_, ones128[:], r_, start=(c == 0), stop=(c == KC - 1),
                                       skip_group_check=True),
                              waits=waits)
                    for h in range(2):
                        rhs = sq[c % 2][:, h * 512:(h + 1) * 512]
                        dst = psq_row[0:1, h * 512:(h + 1) * 512]
                        pl.op("tensor", lambda e, d_=dst, r_=rhs, c=c:
                              e.matmul(d_, ones128[:], r_, start=(c == 0), stop=(c == KC - 1),
                                       skip_group_check=True),
                              inc="Pe" if h == 1 else None)
                    pst_[c] = pl.val("Pe")

            def emit_bcast(tag, psbc):
                pbc = {}
                srcs = [S1[:, 0:512], S1[:, 512:1024], S2[:, 0:512], S2[:, 512:1024]]
                for i in range(4):
                    if i == 0:
                        waits = [("V", R("vfin" + tag, 1))]
                    else:
                        waits = [("A", (abc1 if tag == "1" else abc2)[i - 1])]
                    pl.op("tensor", lambda e, r_=srcs[i]:
                          e.matmul(psbc, ones_row, r_, start=True, stop=True, skip_group_check=True),
                          waits=waits, inc="Pe")
                    pbc[i] = pl.val("Pe")
                tb["pbc" + tag] = pbc

            for cl in range(4):
                emit_pg(cl)
            tb["pg"] = pg
            emit_stats("1", asq1, ps_sum[1], ps_sq[1], range(0, 8))
            for j in range(4):
                emit_ph(j)
            emit_stats("1", asq1, ps_sum[1], ps_sq[1], range(8, 16))
            for j in range(4, 8):
                emit_ph(j)
            tb["ph"] = ph
            emit_stats("1", asq1, ps_sum[1], ps_sq[1], range(16, KC))
            emit_bcast("1", ps_bc[1])
            pmm, pstop3 = {}, {}
            for gi in range(NG):
                for c in range(KC):
                    waits = [("V", vmult[(gi, c)])]
                    if gi == 1 and c == 0:
                        waits.append(("A", tb["ahev"][0]))
                    for ncl in range(2):
                        lhsT = wm[c % 2][:, ncl * 128:(ncl + 1) * 128]
                        pl.op("tensor", lambda e, l_=lhsT: e.ldweights(l_),
                              waits=waits if ncl == 0 else [])
                        for h in range(2):
                            rhs = xbv(c)[:, h * 512:(h + 1) * 512]
                            dst = ps4[pair3(gi)][ncl][:, h * 512:(h + 1) * 512]
                            last = (ncl == 1 and h == 1)
                            pl.op("tensor", lambda e, d_=dst, l_=lhsT, r_=rhs, c=c:
                                  e.matmul(d_, l_, r_, start=(c == 0), stop=(c == KC - 1),
                                           skip_group_check=True),
                                  inc="Pe" if last else None)
                    pmm[(gi, c)] = pl.val("Pe")
                pstop3[gi] = pl.val("Pe")
            tb["pmm"] = pmm
            tb["pstop3"] = pstop3
            emit_stats("2", asq2, ps_sum[2], ps_sq[2], range(KC),
                       first_extra=[("V", R("vev3", (14, 1)))])
            emit_bcast("2", ps_bc[2])
            for c in range(KC):
                waits_e = []
                if c == 0:
                    waits_e.append(("V", R("vnorm2", 0)))
                    waits_e.append(tb["g_enc"])
                    waits_e.append(("V", R("vev3", (NG - 1, 1))))
                else:
                    waits_e.append(("V", R("vnorm2", c)))
                for h in range(2):
                    lhsT = encv(c)
                    rhs = xbv(c)[:, h * 512:(h + 1) * 512]
                    dst = ps_e[:, h * 512:(h + 1) * 512]
                    pl.op("tensor", lambda e, d_=dst, l_=lhsT, r_=rhs, c=c:
                          e.matmul(d_, l_, r_, start=(c == 0), stop=(c == KC - 1), skip_group_check=True),
                          waits=waits_e if h == 0 else [],
                          inc="Pe" if (c == KC - 1 and h == 1) else None)
                waits_f = []
                if c == 0:
                    waits_f.append(gdma6[0])
                    waits_f.append(tb["g_HT"])
                for ncl in range(2):
                    lhsT = wg[0][:, c * 256 + ncl * 128:c * 256 + (ncl + 1) * 128]
                    pl.op("tensor", lambda e, l_=lhsT: e.ldweights(l_),
                          waits=waits_f if ncl == 0 else [])
                    for h in range(2):
                        rhs = xbv(c)[:, h * 512:(h + 1) * 512]
                        dst = ps4[0][ncl][:, h * 512:(h + 1) * 512]
                        pl.op("tensor", lambda e, d_=dst, l_=lhsT, r_=rhs, c=c:
                              e.matmul(d_, l_, r_, start=(c == 0), stop=False,
                                       skip_group_check=True))
            tb["p_e"] = {0: pl.val("Pe")}
            pstop6 = {}
            for gi in range(NG):
                for c in range(KC):
                    if gi == 0:
                        break
                    waits = []
                    if c == 0:
                        waits.append(gdma6[gi])
                        if gi == 1:
                            waits.append(("A", tb["a_eevac"][0]))
                        if gi >= 2:
                            waits.append(("V", R("vev6", (gi - 2, 1))))
                    for ncl in range(2):
                        lhsT = wg[gi % 2][:, c * 256 + ncl * 128:c * 256 + (ncl + 1) * 128]
                        pl.op("tensor", lambda e, l_=lhsT: e.ldweights(l_),
                              waits=waits if ncl == 0 else [])
                        for h in range(2):
                            rhs = xbv(c)[:, h * 512:(h + 1) * 512]
                            dst = ps4[gi % 2][ncl][:, h * 512:(h + 1) * 512]
                            pl.op("tensor", lambda e, d_=dst, l_=lhsT, r_=rhs, c=c:
                                  e.matmul(d_, l_, r_, start=(c == 0), stop=False,
                                           skip_group_check=True))
                for ncl in range(2):
                    nc_ = 2 * gi + ncl
                    lhsT = HT[:, nc_ * P:(nc_ + 1) * P]
                    pl.op("tensor", lambda e, l_=lhsT: e.ldweights(l_),
                          waits=[("A", tb["a_eevac"][0])] if (gi == 0 and ncl == 0) else [])
                    for h in range(2):
                        rhs = e_pad[:, h * 512:(h + 1) * 512]
                        dst = ps4[gi % 2][ncl][:, h * 512:(h + 1) * 512]
                        last = (ncl == 1 and h == 1)
                        pl.op("tensor", lambda e, d_=dst, l_=lhsT, r_=rhs:
                              e.matmul(d_, l_, r_, start=False, stop=True, skip_group_check=True),
                              inc="Pe" if last else None)
                pstop6[gi] = pl.val("Pe")
            tb["pstop6"] = pstop6
            return tb

        tb = plan_main(_Plan(dict(ctr)), None)
        pl = _Plan(ctr)
        plan_main(pl, tb)
        _emit_block(nc, pl, sems, "main")

    dump("dbg_w01", W01)
    dump("dbg_patk", patk[:])
    dump("dbg_HT", HT[:])
    dump("dbg_x2", xa[:])
    dump("dbg_x2l", xb[:])
    dump("dbg_e", e_pad[:])

    _CACHE["nc"] = nc
    return nc


def _sw(a, width):
    return np.ascontiguousarray(a.reshape(KC, P, width).transpose(1, 0, 2).reshape(P, KC * width))


def _prep_shared(w, mask, wb, b, enc0, enc1, c0, c1, m0, m1):
    sh = {}
    wT = np.asarray(w, np.float32).T
    sh["wT"] = np.ascontiguousarray(
        wT.reshape(KC, P, NG, 256).transpose(1, 2, 0, 3).reshape(P, NG * 8192))
    wbT = np.asarray(wb, np.float32).T
    sh["wbT"] = np.ascontiguousarray(
        wbT.reshape(KC, P, NG, 256).transpose(1, 2, 0, 3).reshape(P, NG * 8192))
    sh["maskT_full"] = np.ascontiguousarray(np.asarray(mask, np.int32).T)
    sh["FT_full"] = _rfo_matrix_T()
    sh["b2d"] = np.ascontiguousarray(np.asarray(b, np.float32).reshape(KC, P).T)
    encT = np.concatenate([np.asarray(enc0, np.float32), np.asarray(enc1, np.float32)], axis=0).T
    sh["encT"] = _sw(encT, 40)
    sh["c0T"] = _sw(np.asarray(c0, np.float32).transpose(1, 0, 2).reshape(D, 96), 96)
    sh["c1T"] = _sw(np.asarray(c1, np.float32).transpose(1, 0, 2).reshape(D, 24), 24)
    sh["m0T"] = _sw(np.ascontiguousarray(np.asarray(m0).transpose(1, 0, 2).reshape(D, 96)).view(np.uint8), 96)
    sh["m1T"] = _sw(np.ascontiguousarray(np.asarray(m1).transpose(1, 0, 2).reshape(D, 24)).view(np.uint8), 24)
    sh["g32"] = np.ascontiguousarray(
        np.kron(np.eye(4, dtype=np.float32), np.ones((32, 32), np.float32)).astype(BF))
    return sh


def kernel(L0_attn_data, L0_attn_weight, L0_attn_mask, L0_ffn_W_base, L0_ffn_b_base,
           L0_ffn_enc0, L0_ffn_enc1, L0_ffn_coefs0, L0_ffn_coefs1, L0_ffn_mask0, L0_ffn_mask1):
    x = np.asarray(L0_attn_data, np.float32).reshape(NCORES * T, D)
    sh = _prep_shared(L0_attn_weight, L0_attn_mask, L0_ffn_W_base, L0_ffn_b_base,
                      L0_ffn_enc0, L0_ffn_enc1, L0_ffn_coefs0, L0_ffn_coefs1,
                      L0_ffn_mask0, L0_ffn_mask1)
    maskT_full = sh.pop("maskT_full")
    FT_full = sh.pop("FT_full")
    nc = _build_nc()
    in_maps = []
    for i in range(NCORES):
        m = dict(sh)
        xt = x[i * T:(i + 1) * T, :].T
        m["xT"] = _sw(xt, T)
        ms = maskT_full[i * 512:(i + 1) * 512, :]
        m["maskS"] = np.ascontiguousarray(ms.reshape(4, P, D).transpose(1, 0, 2).reshape(P, 4 * D))
        fs = FT_full[:, i * 512:(i + 1) * 512]
        m["FTS"] = _sw(fs, 512)
        in_maps.append(m)
    res = run_bass_kernel_spmd(nc, in_maps, core_ids=list(range(NCORES)))
    out = np.empty((NCORES * T, D), np.float32)
    for i in range(NCORES):
        out[i * T:(i + 1) * T, :] = res.results[i]["out"].T
    return out.reshape(4, 2048, D)


# version 21
# speedup vs baseline: 1.0796x; 1.0796x over previous
import numpy as np
import ml_dtypes
import concourse.bass as bass
import concourse.mybir as mybir
from concourse.bass_utils import run_bass_kernel_spmd

F32 = mybir.dt.float32
BF16 = mybir.dt.bfloat16
I32 = mybir.dt.int32
U8 = mybir.dt.uint8
AF = mybir.ActivationFunctionType
ALU = mybir.AluOpType
AX = mybir.AxisListType
BF = ml_dtypes.bfloat16

P = 128
D = 4096
T = 1024
KC = 32
NG = 16
NCORES = 8
EPS = 1e-5

_CACHE = {}
_DEBUG = False


def _rfo_matrix_T():
    if "FT" in _CACHE:
        return _CACHE["FT"]
    n = D
    eye = np.eye(n, dtype=np.float64)
    max_f = (n - 1) // 2
    spec = np.zeros((n, n // 2 + 1), dtype=np.complex128)
    spec[:, 0] = eye[:, 0] * np.sqrt(float(n))
    spec[:, 1:1 + max_f] = (eye[:, 1:1 + 2 * max_f:2] - 1j * eye[:, 2:2 + 2 * max_f:2]) * np.sqrt(n / 2.0)
    spec[:, n // 2] = eye[:, -1] * np.sqrt(float(n))
    ft = np.fft.irfft(spec, n=n, axis=-1)
    ft = np.ascontiguousarray(ft.astype(BF))
    _CACHE["FT"] = ft
    return ft


class _Plan:
    def __init__(self, ctr):
        self.streams = {e: [] for e in ("sync", "gpsimd", "vector", "scalar", "tensor")}
        self.ctr = ctr

    def val(self, sem):
        return self.ctr.get(sem, 0)

    def op(self, eng, fn, waits=(), inc=None, inc_n=1):
        if inc is not None:
            self.ctr[inc] = self.ctr.get(inc, 0) + inc_n
        self.streams[eng].append((list(waits), fn, inc, inc_n))
        return self.ctr.get(inc, 0) if inc is not None else None

    def wait(self, eng, waits):
        self.streams[eng].append((list(waits), None, None, 0))


def _emit_block(nc, plan, sems, name):
    with nc.Block(no_gpsimd_drain=True) as blk:
        for eng_name in ("sync", "gpsimd", "vector", "scalar", "tensor"):
            ops = plan.streams[eng_name]
            if not ops:
                continue

            def mk(ops):
                def body(eng):
                    for waits, fn, inc, inc_n in ops:
                        for (s, v) in waits:
                            if v > 0:
                                eng.wait_ge(sems[s], v)
                        if fn is None:
                            continue
                        ins = fn(eng)
                        if inc is not None:
                            ins.then_inc(sems[inc], inc_n)
                return body

            getattr(blk, eng_name)(mk(ops))


def _build_nc():
    if "nc" in _CACHE:
        return _CACHE["nc"]
    from contextlib import ExitStack

    nc = bass.Bass("TRN2", target_bir_lowering=False, debug=False)

    d_xT = nc.dram_tensor("xT", [P, KC * T], F32, kind="ExternalInput").ap()
    d_wT = nc.dram_tensor("wT", [P, NG * 8192], F32, kind="ExternalInput").ap()
    d_maskS = nc.dram_tensor("maskS", [P, 4 * D], I32, kind="ExternalInput").ap()
    d_wbT = nc.dram_tensor("wbT", [P, NG * 8192], F32, kind="ExternalInput").ap()
    d_FTS = nc.dram_tensor("FTS", [P, 32 * 512], BF16, kind="ExternalInput").ap()
    d_b2d = nc.dram_tensor("b2d", [P, KC], F32, kind="ExternalInput").ap()
    d_encT = nc.dram_tensor("encT", [P, 1280], F32, kind="ExternalInput").ap()
    d_c0T = nc.dram_tensor("c0T", [P, 3072], F32, kind="ExternalInput").ap()
    d_c1T = nc.dram_tensor("c1T", [P, 768], F32, kind="ExternalInput").ap()
    d_m0T = nc.dram_tensor("m0T", [P, 3072], U8, kind="ExternalInput").ap()
    d_m1T = nc.dram_tensor("m1T", [P, 768], U8, kind="ExternalInput").ap()
    d_g32 = nc.dram_tensor("g32", [P, P], BF16, kind="ExternalInput").ap()
    d_out = nc.dram_tensor("out", [D, T], F32, kind="ExternalOutput").ap()
    cc_pin = nc.dram_tensor("cc_pin", [512, P], BF16, kind="Internal").ap()
    cc_pout = nc.dram_tensor("cc_pout", [D, P], BF16, kind="Internal").ap()
    cc_hin = nc.dram_tensor("cc_hin", [40, 512], BF16, kind="Internal").ap()
    cc_hout = nc.dram_tensor("cc_hout", [320, 512], BF16, kind="Internal").ap()

    dbg = {}
    if _DEBUG:
        for nm, shape in [("dbg_patk", [P, KC * P]), ("dbg_HT", [P, D]),
                          ("dbg_x2", [P, KC * T]), ("dbg_x2l", [P, KC * T]),
                          ("dbg_e", [P, T]), ("dbg_w01", [P, 1280])]:
            dbg[nm] = nc.dram_tensor(nm, shape, BF16, kind="ExternalOutput").ap()

    st = ExitStack()
    nc._keepalive = st

    def sb(name, shape, dt):
        return st.enter_context(nc.sbuf_tensor(name, shape, dt))

    xa = sb("xa", [P, KC * T], BF16)
    xb = sb("xb", [P, KC * T], BF16)
    wg = [sb("wg0", [P, 8192], BF16), sb("wg1", [P, 8192], BF16)]
    wm = [sb("wm0", [P, 256], BF16), sb("wm1", [P, 256], BF16)]
    patk = sb("patk", [P, KC * P], BF16)
    HT = sb("HT", [P, D], BF16)
    e_pad = sb("e_pad", [P, T], BF16)
    b2d = sb("b2d_sb", [P, KC], F32)
    ones128 = sb("ones128", [P, 1], BF16)
    tmpbf = [sb("tmpbf0", [P, T], BF16), sb("tmpbf1", [P, T], BF16)]
    outb = [sb("outb0", [P, T], F32), sb("outb1", [P, T], F32)]
    rows = sb("rows", [1, 2180], F32)
    sqrb = sb("sqrb", [P, 4096], BF16)
    eps_c = rows[0:1, 2176:2177]

    g32 = patk[:, 3840:3968]
    m0st = wg[0][:, 0:3072]
    m1st = wg[0][:, 3072:3840]
    W01 = wg[0][:, 4096:5376]
    mhalfX = [xb[:, 0:2048], xb[:, 2048:4096]]
    ftile = [wg[0][:, 5376:7424], wg[1][:, 0:2048]]
    red = [wg[0][:, 7424 + 128 * i:7552 + 128 * i] for i in range(4)]
    patk_loc = wg[1][:, 2048:2560]
    c0st = wg[1][:].bitcast(F32)[:, 0:3072]
    c1st = wg[1][:].bitcast(F32)[:, 3072:3840]
    sq = [sqrb[:, 0:1024], sqrb[:, 1024:2048]]
    rB_bf = sqrb[:, 2048:3072]
    mrB_bf = sqrb[:, 3072:4096]
    encvB = wm[0][:, 0:240]
    encvC = wm[1][:, 0:40]

    S0 = rows[0:1, 0:1024]
    ones_row = rows[0:1, 1024:1152]
    Trow = rows[0:1, 1152:2176]
    S1 = outb[1][0:1, 0:1024]
    S2 = outb[0][0:1, 0:1024]

    def xav(c):
        return xa[:, c * T:(c + 1) * T]

    def xbv(c):
        return xb[:, c * T:(c + 1) * T]

    def encv(c):
        if c < 25:
            return patk[:, c * 40:(c + 1) * 40]
        if c < 31:
            return encvB[:, (c - 25) * 40:(c - 24) * 40]
        return encvC

    sems = {}
    for name in ("S", "M", "X1", "X2", "X3", "X4", "Ka", "Kb", "P1", "Ca", "Cb", "H1",
                 "E", "Wa", "Wb", "Wc", "Fa", "Fb", "Oa", "Ob", "Pe", "V", "A"):
        sems[name] = st.enter_context(nc.semaphore("sem_" + name))
    ctr = {}

    nc.const_aps.register(F32, EPS, eps_c)

    def dump(name, src_ap):
        if not _DEBUG:
            return
        pl = _Plan(ctr)
        pl.op("sync", lambda e: e.dma_start(dbg[name], src_ap), inc="S", inc_n=16)
        pl.wait("sync", [("S", pl.val("S"))])
        _emit_block(nc, pl, sems, "dump_" + name)


    with ExitStack() as pst:
        ps4 = [[pst.enter_context(nc.psum_tensor(f"ps{i}{j}", [P, T], F32))
                for j in range(2)] for i in range(2)]
        ps_pat = [ps4[1][1][:, 0:128], ps4[1][1][:, 512:640]]
        ps_H = ps4[1][0][0:40, 512:1024]
        ps_sum = {1: ps4[0][0][0:1, 0:1024], 2: ps4[0][0][0:1, 0:1024]}
        ps_sq = {1: ps4[0][1][0:1, 0:1024], 2: ps4[0][1][0:1, 0:1024]}
        ps_bc = {1: ps4[1][0][:, 0:512], 2: ps4[0][0][:, 0:512]}
        ps_e = ps4[1][0][0:40, 0:1024]

        def pair3(gi):
            return gi % 2

        def plan_main(pl, rs):
            tb = {}

            def R(*keys):
                d = rs
                if d is None:
                    return 0
                for k in keys:
                    d = d[k]
                return d

            pl.op("sync", lambda e: e.dma_start(b2d[:], d_b2d), inc="S", inc_n=16)
            pl.op("sync", lambda e: e.dma_start(g32, d_g32), inc="S", inc_n=16)
            pl.op("sync", lambda e: e.dma_start(c0st, d_c0T), inc="S", inc_n=16)
            pl.op("sync", lambda e: e.dma_start(c1st, d_c1T), inc="S", inc_n=16)
            s_small = pl.val("S")
            pl.op("gpsimd", lambda e: e.dma_start(m0st, d_m0T), inc="M", inc_n=16)
            pl.op("gpsimd", lambda e: e.dma_start(m1st, d_m1T), inc="M", inc_n=16)
            gm = {}

            def emit_mask(i):
                cl, hf = divmod(i, 2)
                waits = [("V", R("vr", i - 2))] if i >= 2 else []
                sk = "Ka" if i % 2 == 0 else "Kb"
                src = d_maskS[:, cl * D + hf * 2048:cl * D + (hf + 1) * 2048]
                pl.op("gpsimd", lambda e, s_=src, d_=mhalfX[i % 2]: e.dma_start(d_, s_),
                      waits=waits, inc=sk, inc_n=16)
                gm[i] = (sk, pl.val(sk))

            def emit_xa(i):
                src_ = d_xT[:, i * 8 * T:(i + 1) * 8 * T]
                dst_ = xa[:, i * 8 * T:(i + 1) * 8 * T]
                sx = ("X1", "X2", "X3", "X4")[i]
                pl.op("gpsimd", lambda e, s_=src_, d_=dst_: e.dma_start(d_, s_), inc=sx, inc_n=16)

            for i in range(6):
                emit_mask(i)
            emit_xa(0)
            emit_mask(6)
            emit_mask(7)
            tb["gm"] = gm
            for i in range(1, 4):
                emit_xa(i)
            pl.op("vector", lambda e: e.memset(eps_c, EPS))
            pl.op("vector", lambda e: e.memset(ones_row, 1.0))
            pl.op("vector", lambda e: e.memset(HT[:], 0.0))
            pl.op("vector", lambda e: e.memset(e_pad[:], 0.0))
            pl.op("vector", lambda e: e.memset(ones128[:], 1.0), inc="V")
            v_memset = pl.val("V")
            tb["v_memset"] = v_memset
            prod0 = patk[:, 0:3072]
            prod1 = patk[:, 3072:3840]
            p0v = prod0.rearrange("p (c e r) -> p c e r", e=3, r=32)
            p1v = prod1.rearrange("p (c e r) -> p c e r", e=3, r=8)
            s1v = tmpbf[0][:, 0:1024].rearrange("p (c r) -> p c r", r=32)
            s2v = tmpbf[1][:, 0:256].rearrange("p (c r) -> p c r", r=8)
            w01v = W01.rearrange("p (c r) -> p c r", r=40)
            pl.op("vector", lambda e: e.tensor_tensor(prod0, c0st, m0st, ALU.mult),
                  waits=[("S", s_small), ("M", 32), ("V", pl.val("V"))], inc="V")
            pl.op("vector", lambda e: e.tensor_tensor(s1v, p0v[:, :, 0:1, :].squeeze(2), p0v[:, :, 1:2, :].squeeze(2), ALU.add),
                  waits=[("V", pl.val("V"))], inc="V")
            pl.op("vector", lambda e: e.tensor_tensor(w01v[:, :, 0:32], s1v, p0v[:, :, 2:3, :].squeeze(2), ALU.add),
                  waits=[("V", pl.val("V"))], inc="V")
            pl.op("vector", lambda e: e.tensor_tensor(prod1, c1st, m1st, ALU.mult),
                  waits=[("V", pl.val("V"))], inc="V")
            pl.op("vector", lambda e: e.tensor_tensor(s2v, p1v[:, :, 0:1, :].squeeze(2), p1v[:, :, 1:2, :].squeeze(2), ALU.add),
                  waits=[("V", pl.val("V"))], inc="V")
            pl.op("vector", lambda e: e.tensor_tensor(w01v[:, :, 32:40], s2v, p1v[:, :, 2:3, :].squeeze(2), ALU.add),
                  waits=[("V", pl.val("V"))], inc="V")
            tb["v_w01"] = pl.val("V")

            pl.op("gpsimd", lambda e: e.dma_start(
                cc_pin.rearrange("(cl p) nb -> p cl nb", p=P),
                patk_loc.rearrange("p (cl nb) -> p cl nb", nb=P)),
                waits=[("V", R("vm", 3))], inc="P1", inc_n=16)
            pl.op("gpsimd", lambda e: e.collective_compute(
                "AllGather", ALU.bypass, replica_groups=[list(range(NCORES))],
                ins=[cc_pin.opt()], outs=[cc_pout.opt()]),
                waits=[("P1", 16)], inc="Ca")
            pl.op("gpsimd", lambda e: e.dma_start(
                patk[:].rearrange("p (c nb) -> p c nb", nb=P),
                cc_pout.rearrange("(c p) nb -> p c nb", p=P)),
                waits=[("Ca", 1)], inc="P1", inc_n=16)
            tb["g_patk"] = ("P1", 32)
            gdma3 = {}
            g0keys = {}
            g0waits = [
                [("V", R("v_w01"))],
                [("V", R("v_w01"))],
                [("Pe", R("ph", 7)), ("V", R("vr", 6))],
                [("Pe", R("ph", 7)), ("Pe", R("pg", 3)), ("V", R("vr", 7))],
            ]
            for q in range(4):
                sk = ("Wa", "Wc", "Ka", "Kb")[q]
                src = d_wT[:, q * 2048:(q + 1) * 2048]
                dst = wg[0][:, q * 2048:(q + 1) * 2048]
                pl.op("gpsimd", lambda e, s_=src, d_=dst: e.dma_start(d_, s_),
                      waits=g0waits[q], inc=sk, inc_n=16)
                g0keys[q] = (sk, pl.val(sk))
            gdma3[0] = g0keys[0]
            tb["g0keys"] = g0keys
            pl.op("gpsimd", lambda e: e.dma_start(wg[1][:, 0:4096], d_wT[:, 8192:12288]),
                waits=[("Pe", R("ph", 7)), ("V", R("v_w01")), ("P1", 16)], inc="Wb", inc_n=16)
            gdma3[1] = ("Wb", pl.val("Wb"))
            pl.op("gpsimd", lambda e: e.dma_start(wg[1][:, 4096:8192], d_wT[:, 12288:16384]),
                waits=[("V", R("vmult", (1, 0)))], inc="Wb", inc_n=16)
            tb["g1b"] = ("Wb", pl.val("Wb"))
            g3b = {}
            for gi in range(2, NG):
                sw = "Wa" if gi % 2 == 0 else "Wb"
                src = d_wT[:, gi * 8192:(gi + 1) * 8192]
                pl.op("gpsimd", lambda e, s_=src[:, 0:4096], d_=wg[gi % 2][:, 0:4096]:
                      e.dma_start(d_, s_),
                      waits=[("V", R("vmult", (gi - 2, KC - 1)))], inc=sw, inc_n=16)
                gdma3[gi] = (sw, pl.val(sw))
                pl.op("gpsimd", lambda e, s_=src[:, 4096:8192], d_=wg[gi % 2][:, 4096:8192]:
                      e.dma_start(d_, s_),
                      waits=[("V", R("vmult", (gi, 0)))], inc=sw, inc_n=16)
                g3b[gi] = (sw, pl.val(sw))
            tb["gdma3"] = gdma3
            tb["g3b"] = g3b
            gdma6 = {}
            for gi in range(NG):
                if gi == 0:
                    waits = [("V", R("vmult", (14, KC - 1)))]
                elif gi == 1:
                    waits = [("V", R("vmult", (15, KC - 1)))]
                else:
                    waits = [("Pe", R("pstop6", gi - 2))]
                sw = "Wa" if gi % 2 == 0 else "Wb"
                src = d_wbT[:, gi * 8192:(gi + 1) * 8192]
                pl.op("gpsimd", lambda e, s_=src, d_=wg[gi % 2][:]: e.dma_start(d_, s_),
                      waits=waits, inc=sw, inc_n=16)
                gdma6[gi] = (sw, pl.val(sw))
                if gi == 1:
                    pl.op("gpsimd", lambda e: e.dma_start(patk[:, 0:1000], d_encT[:, 0:1000]),
                        waits=[("V", R("vmult", (NG - 1, 7)))], inc="E", inc_n=16)
                    pl.op("gpsimd", lambda e: e.dma_start(encvB, d_encT[:, 1000:1240]),
                        waits=[("Pe", R("pmm", (NG - 1, KC - 1)))], inc="E", inc_n=16)
                    pl.op("gpsimd", lambda e: e.dma_start(encvC, d_encT[:, 1240:1280]),
                        inc="E", inc_n=16)
                    tb["g_enc"] = ("E", 48)
                    pl.op("gpsimd", lambda e: e.dma_start(cc_hin, HT[0:40, 0:512]),
                          waits=[("A", R("ahev", 0))], inc="H1", inc_n=16)
                    pl.op("gpsimd", lambda e: e.collective_compute(
                        "AllGather", ALU.bypass, replica_groups=[list(range(NCORES))],
                        ins=[cc_hin.opt()], outs=[cc_hout.opt()]),
                        waits=[("H1", 16)], inc="Cb")
                    pl.op("gpsimd", lambda e: e.dma_start(
                        HT[0:40, :].rearrange("p (r u) -> p r u", u=512),
                        cc_hout.rearrange("(r p) u -> p r u", p=40)),
                        waits=[("Cb", 1)], inc="H1", inc_n=16)
                    tb["g_HT"] = ("H1", 32)
            tb["gdma6"] = gdma6

            sf = {}
            for j in range(8):
                waits = [("Pe", R("ph", j - 2))] if j >= 2 else [("V", R("v_w01"))]
                sfx = "Fa" if j % 2 == 0 else "Fb"
                src = d_FTS[:, j * 2048:(j + 1) * 2048]
                pl.op("sync", lambda e, s_=src, d_=ftile[j % 2]: e.dma_start(d_, s_),
                      waits=waits, inc=sfx, inc_n=16)
                sf[j] = (sfx, pl.val(sfx))
            tb["sf"] = sf
            odma = {}
            for gi in range(NG):
                for ncl in range(2):
                    nc_ = 2 * gi + ncl
                    so = "Oa" if ncl == 0 else "Ob"
                    dst = d_out[nc_ * P:(nc_ + 1) * P, :]
                    pl.op("sync", lambda e, d_=dst, ncl=ncl: e.dma_start(d_, outb[ncl][:]),
                          waits=[("V", R("vev6", (gi, ncl)))], inc=so, inc_n=16)
                    odma[(gi, ncl)] = (so, pl.val(so))
            tb["odma"] = odma
            pl.wait("sync", [("Oa", pl.val("Oa")), ("Ob", pl.val("Ob"))])

            asq1 = {}
            for c in range(KC):
                waits = [(("X1", "X2", "X3", "X4")[c // 8], 16)]
                if c >= 2:
                    waits.append(("Pe", R("pst1", c - 2)))
                pl.op("scalar", lambda e, c=c: e.activation(sq[c % 2], xav(c), AF.Square),
                      waits=waits, inc="A")
                asq1[c] = pl.val("A")
            tb["asq1"] = asq1
            pl.op("scalar", lambda e: e.activation(S2, S1, AF.Sqrt, bias=EPS),
                  waits=[("V", R("vfin1", 0))], inc="A")
            tb["asrt1"] = pl.val("A")
            abc1 = {}
            for i, dst in enumerate([rB_bf[:, 0:512], rB_bf[:, 512:1024],
                                     mrB_bf[:, 0:512], mrB_bf[:, 512:1024]]):
                pl.op("scalar", lambda e, d_=dst: e.activation(d_, ps_bc[1], AF.Copy),
                      waits=[("Pe", R("pbc1", i))], inc="A")
                abc1[i] = pl.val("A")
            tb["abc1"] = abc1
            pl.op("scalar", lambda e: e.activation(HT[0:40, 0:512], ps_H, AF.Copy, scale=0.5),
                  waits=[("Pe", R("ph", 7)), ("V", R("v_memset"))], inc="A")
            tb["ahev"] = {0: pl.val("A")}
            acp, asq2 = {}, {}
            for gi in range(NG):
                for ncl in range(2):
                    nc_ = 2 * gi + ncl
                    pl.op("scalar", lambda e, ncl=ncl, nc_=nc_:
                          e.activation(xav(nc_), tmpbf[ncl][:], AF.Copy),
                          waits=[("V", R("vev3", (gi, ncl)))], inc="A")
                    acp[(gi, ncl)] = pl.val("A")
            for c in range(KC):
                waits = [("A", acp[(c // 2, c % 2)])]
                if c >= 2:
                    waits.append(("Pe", R("pst2", c - 2)))
                pl.op("scalar", lambda e, c=c: e.activation(sq[c % 2], xav(c), AF.Square),
                      waits=waits, inc="A")
                asq2[c] = pl.val("A")
            tb["acp"] = acp
            tb["asq2"] = asq2
            pl.op("scalar", lambda e: e.activation(S2, S1, AF.Sqrt, bias=EPS),
                  waits=[("V", R("vfin2", 0))], inc="A")
            tb["asrt2"] = pl.val("A")
            abc2 = {}
            for i, dst in enumerate([rB_bf[:, 0:512], rB_bf[:, 512:1024],
                                     mrB_bf[:, 0:512], mrB_bf[:, 512:1024]]):
                pl.op("scalar", lambda e, d_=dst: e.activation(d_, ps_bc[2], AF.Copy),
                      waits=[("Pe", R("pbc2", i))], inc="A")
                abc2[i] = pl.val("A")
            tb["abc2"] = abc2
            pl.op("scalar", lambda e: e.activation(e_pad[0:40, :], ps_e, AF.Copy),
                  waits=[("Pe", R("p_e", 0)), ("V", R("v_memset"))], inc="A")
            tb["a_eevac"] = {0: pl.val("A")}

            vr, vm = {}, {}

            def emit_reduce(i):
                cl, hf = divmod(i, 2)
                waits = [gm[i]]
                src = mhalfX[i % 2].rearrange("p (a b) -> p a b", b=32)
                dst = red[cl][:, hf * 64:(hf + 1) * 64]

                def do_reduce(e, s_=src, d_=dst):
                    with nc.allow_low_precision(reason="exact small-int blocksum"):
                        return e.tensor_reduce(d_, s_, AX.X, ALU.add)

                pl.op("vector", do_reduce, waits=waits + [("V", pl.val("V"))], inc="V")
                vr[i] = pl.val("V")

            def emit_min(cl):
                dst = patk_loc[:, cl * P:(cl + 1) * P]
                pl.op("vector", lambda e, s_=ps_pat[cl % 2], d_=dst: e.tensor_scalar_min(d_, s_, 1.0),
                      waits=[("Pe", R("pg", cl)), ("V", pl.val("V"))], inc="V")
                vm[cl] = pl.val("V")

            for it in [0, 1, 2, 3, 4, 5, 6, 7, (0,), (1,), (2,), (3,)]:
                if isinstance(it, tuple):
                    emit_min(it[0])
                else:
                    emit_reduce(it)
            tb["vr"] = vr
            tb["vm"] = vm

            def emit_finish(tag, psum_row, psq_row, pst_last, asrt_count):
                pl.op("vector", lambda e: e.tensor_scalar_mul(S0, psum_row, 1.0 / D),
                      waits=[("Pe", pst_last), ("V", pl.val("V"))], inc="V")
                pl.op("vector", lambda e: e.tensor_tensor(S2, S0, S0, ALU.mult),
                      waits=[("V", pl.val("V"))], inc="V")
                pl.op("vector", lambda e: e.scalar_tensor_tensor(S1, psq_row, 1.0 / D, S2,
                                                                ALU.mult, ALU.subtract),
                      waits=[("V", pl.val("V"))], inc="V")
                tb["vfin" + tag] = {0: pl.val("V")}
                pl.op("vector", lambda e: e.reciprocal(S1, S2), waits=[("A", asrt_count)], inc="V")
                pl.op("vector", lambda e: e.tensor_tensor(S2, S0, S1, ALU.mult),
                      waits=[("V", pl.val("V"))], inc="V")
                tb["vfin" + tag][1] = pl.val("V")

            def norm_pair(c, extra):
                pl.op("vector", lambda e, c=c: e.tensor_tensor(tmpbf[0][:], xav(c), rB_bf, ALU.mult),
                      waits=list(extra) + [("V", pl.val("V"))], inc="V")
                pl.op("vector", lambda e, c=c: e.tensor_tensor(xbv(c), tmpbf[0][:], mrB_bf, ALU.subtract),
                      waits=[("V", pl.val("V"))], inc="V")
                return pl.val("V")

            emit_finish("1", ps_sum[1], ps_sq[1], R("pst1", KC - 1), tb["asrt1"])
            vmult, vnorm1 = {}, {}

            def emit_mult(gi, c):
                waits = []
                if gi == 0:
                    if c % 8 == 0:
                        waits.append(tb["g0keys"][c // 8])
                    if c == 0:
                        waits.append(tb["g_patk"])
                elif c == 0:
                    waits.append(gdma3[gi])
                elif c == 16:
                    if gi == 1:
                        waits.append(tb["g1b"])
                    else:
                        waits.append(tb["g3b"][gi])
                key = (gi, c - 2) if c >= 2 else ((gi - 1, KC - 2 + c) if gi >= 1 else None)
                if key is not None:
                    waits.append(("Pe", R("pmm", key)))
                w_in = wg[gi % 2][:, c * 256:(c + 1) * 256].rearrange("p (a b) -> p a b", b=32)
                p_in = patk[:, c * P + gi * 8:c * P + gi * 8 + 8].unsqueeze(2).broadcast_to([P, 8, 32])
                w_out = wm[c % 2][:].rearrange("p (a b) -> p a b", b=32)
                pl.op("vector", lambda e, a=w_out, x=w_in, y=p_in: e.tensor_tensor(a, x, y, ALU.mult),
                      waits=waits, inc="V")
                vmult[(gi, c)] = pl.val("V")

            for c in range(KC):
                vnorm1[c] = norm_pair(c, [("A", abc1[3])] if c == 0 else [])
                emit_mult(0, c)
            tb["vnorm1"] = vnorm1

            vev3 = {}

            def emit_evac3(gi):
                for ncl in range(2):
                    waits = [("Pe", R("pstop3", gi))]
                    if gi >= 1:
                        waits.append(("A", R("acp", (gi - 1, ncl))))
                    nc_ = 2 * gi + ncl
                    pl.op("vector", lambda e, gi=gi, ncl=ncl, nc_=nc_:
                          e.tensor_tensor(tmpbf[ncl][:], ps4[pair3(gi)][ncl][:], xav(nc_), ALU.add),
                          waits=waits, inc="V")
                    vev3[(gi, ncl)] = pl.val("V")

            for gi in range(1, NG):
                for c in range(KC):
                    emit_mult(gi, c)
                emit_evac3(gi - 1)
            emit_evac3(NG - 1)
            tb["vmult"] = vmult
            tb["vev3"] = vev3

            emit_finish("2", ps_sum[2], ps_sq[2], R("pst2", KC - 1), tb["asrt2"])
            vnorm2 = {}
            for c in range(KC):
                extra = []
                if c == 0:
                    extra = [("A", abc2[3]), ("Pe", R("pstop3", NG - 1))]
                vnorm2[c] = norm_pair(c, extra)
            tb["vnorm2"] = vnorm2
            vev6 = {}
            for gi in range(NG):
                for ncl in range(2):
                    nc_ = 2 * gi + ncl
                    waits = [("Pe", R("pstop6", gi))]
                    if gi >= 1:
                        waits.append(("Oa" if ncl == 0 else "Ob", 16 * gi))
                    pl.op("vector", lambda e, gi=gi, ncl=ncl, nc_=nc_:
                          e.scalar_tensor_tensor(outb[ncl][:], ps4[gi % 2][ncl][:],
                                                 b2d[:, nc_:nc_ + 1], xav(nc_),
                                                 ALU.add, ALU.add),
                          waits=waits + [("V", pl.val("V"))], inc="V")
                    vev6[(gi, ncl)] = pl.val("V")
            tb["vev6"] = vev6

            pg, ph = {}, {}

            def emit_pg(cl):
                waits = [("V", vr[2 * cl + 1])]
                if cl >= 2:
                    waits.append(("V", R("vm", cl - 2)))
                pl.op("tensor", lambda e, cl=cl: e.matmul(ps_pat[cl % 2], g32, red[cl],
                                                          start=True, stop=True, skip_group_check=True),
                      waits=waits, inc="Pe")
                pg[cl] = pl.val("Pe")

            def emit_ph(j):
                for cl4 in range(4):
                    co = 4 * j + cl4
                    lhsT = W01[:, co * 40:(co + 1) * 40]
                    rhs = ftile[j % 2][:, cl4 * 512:(cl4 + 1) * 512]
                    pl.op("tensor", lambda e, l_=lhsT, r_=rhs, co=co:
                          e.matmul(ps_H, l_, r_, start=(co == 0), stop=(co == 31),
                                   skip_group_check=True),
                          waits=[sf[j]] if cl4 == 0 else [],
                          inc="Pe" if cl4 == 3 else None)
                ph[j] = pl.val("Pe")

            def emit_stats(tag, asq, psum_row, psq_row, c_range, first_extra=()):
                pst_ = tb.setdefault("pst" + tag, {})
                for c in c_range:
                    for h in range(2):
                        rhs = xav(c)[:, h * 512:(h + 1) * 512]
                        dst = psum_row[0:1, h * 512:(h + 1) * 512]
                        waits = [("A", asq[c])] if h == 0 else []
                        if c == 0 and h == 0:
                            waits = waits + list(first_extra)
                        pl.op("tensor", lambda e, d_=dst, r_=rhs, c=c:
                              e.matmul(d_, ones128[:], r_, start=(c == 0), stop=(c == KC - 1),
                                       skip_group_check=True),
                              waits=waits)
                    for h in range(2):
                        rhs = sq[c % 2][:, h * 512:(h + 1) * 512]
                        dst = psq_row[0:1, h * 512:(h + 1) * 512]
                        pl.op("tensor", lambda e, d_=dst, r_=rhs, c=c:
                              e.matmul(d_, ones128[:], r_, start=(c == 0), stop=(c == KC - 1),
                                       skip_group_check=True),
                              inc="Pe" if h == 1 else None)
                    pst_[c] = pl.val("Pe")

            def emit_bcast(tag, psbc):
                pbc = {}
                srcs = [S1[:, 0:512], S1[:, 512:1024], S2[:, 0:512], S2[:, 512:1024]]
                for i in range(4):
                    if i == 0:
                        waits = [("V", R("vfin" + tag, 1))]
                    else:
                        waits = [("A", (abc1 if tag == "1" else abc2)[i - 1])]
                    pl.op("tensor", lambda e, r_=srcs[i]:
                          e.matmul(psbc, ones_row, r_, start=True, stop=True, skip_group_check=True),
                          waits=waits, inc="Pe")
                    pbc[i] = pl.val("Pe")
                tb["pbc" + tag] = pbc

            for cl in range(4):
                emit_pg(cl)
            tb["pg"] = pg
            emit_stats("1", asq1, ps_sum[1], ps_sq[1], range(0, 8))
            for j in range(4):
                emit_ph(j)
            emit_stats("1", asq1, ps_sum[1], ps_sq[1], range(8, 16))
            for j in range(4, 8):
                emit_ph(j)
            tb["ph"] = ph
            emit_stats("1", asq1, ps_sum[1], ps_sq[1], range(16, KC))
            emit_bcast("1", ps_bc[1])
            pmm, pstop3 = {}, {}
            for gi in range(NG):
                for c in range(KC):
                    waits = [("V", vmult[(gi, c)])]
                    if gi == 1 and c == 0:
                        waits.append(("A", tb["ahev"][0]))
                    for ncl in range(2):
                        lhsT = wm[c % 2][:, ncl * 128:(ncl + 1) * 128]
                        pl.op("tensor", lambda e, l_=lhsT: e.ldweights(l_),
                              waits=waits if ncl == 0 else [])
                        for h in range(2):
                            rhs = xbv(c)[:, h * 512:(h + 1) * 512]
                            dst = ps4[pair3(gi)][ncl][:, h * 512:(h + 1) * 512]
                            last = (ncl == 1 and h == 1)
                            pl.op("tensor", lambda e, d_=dst, l_=lhsT, r_=rhs, c=c:
                                  e.matmul(d_, l_, r_, start=(c == 0), stop=(c == KC - 1),
                                           skip_group_check=True),
                                  inc="Pe" if last else None)
                    pmm[(gi, c)] = pl.val("Pe")
                pstop3[gi] = pl.val("Pe")
            tb["pmm"] = pmm
            tb["pstop3"] = pstop3
            emit_stats("2", asq2, ps_sum[2], ps_sq[2], range(KC),
                       first_extra=[("V", R("vev3", (14, 1)))])
            emit_bcast("2", ps_bc[2])
            for c in range(KC):
                waits_e = []
                if c == 0:
                    waits_e.append(("V", R("vnorm2", 0)))
                    waits_e.append(tb["g_enc"])
                    waits_e.append(("V", R("vev3", (NG - 1, 1))))
                else:
                    waits_e.append(("V", R("vnorm2", c)))
                for h in range(2):
                    lhsT = encv(c)
                    rhs = xbv(c)[:, h * 512:(h + 1) * 512]
                    dst = ps_e[:, h * 512:(h + 1) * 512]
                    pl.op("tensor", lambda e, d_=dst, l_=lhsT, r_=rhs, c=c:
                          e.matmul(d_, l_, r_, start=(c == 0), stop=(c == KC - 1), skip_group_check=True),
                          waits=waits_e if h == 0 else [],
                          inc="Pe" if (c == KC - 1 and h == 1) else None)
                waits_f = []
                if c == 0:
                    waits_f.append(gdma6[0])
                    waits_f.append(tb["g_HT"])
                for ncl in range(2):
                    lhsT = wg[0][:, c * 256 + ncl * 128:c * 256 + (ncl + 1) * 128]
                    pl.op("tensor", lambda e, l_=lhsT: e.ldweights(l_),
                          waits=waits_f if ncl == 0 else [])
                    for h in range(2):
                        rhs = xbv(c)[:, h * 512:(h + 1) * 512]
                        dst = ps4[0][ncl][:, h * 512:(h + 1) * 512]
                        pl.op("tensor", lambda e, d_=dst, l_=lhsT, r_=rhs, c=c:
                              e.matmul(d_, l_, r_, start=(c == 0), stop=False,
                                       skip_group_check=True))
            tb["p_e"] = {0: pl.val("Pe")}
            pstop6 = {}
            for gi in range(NG):
                for c in range(KC):
                    if gi == 0:
                        break
                    waits = []
                    if c == 0:
                        waits.append(gdma6[gi])
                        if gi == 1:
                            waits.append(("A", tb["a_eevac"][0]))
                        if gi >= 2:
                            waits.append(("V", R("vev6", (gi - 2, 1))))
                    for ncl in range(2):
                        lhsT = wg[gi % 2][:, c * 256 + ncl * 128:c * 256 + (ncl + 1) * 128]
                        pl.op("tensor", lambda e, l_=lhsT: e.ldweights(l_),
                              waits=waits if ncl == 0 else [])
                        for h in range(2):
                            rhs = xbv(c)[:, h * 512:(h + 1) * 512]
                            dst = ps4[gi % 2][ncl][:, h * 512:(h + 1) * 512]
                            pl.op("tensor", lambda e, d_=dst, l_=lhsT, r_=rhs, c=c:
                                  e.matmul(d_, l_, r_, start=(c == 0), stop=False,
                                           skip_group_check=True))
                for ncl in range(2):
                    nc_ = 2 * gi + ncl
                    lhsT = HT[:, nc_ * P:(nc_ + 1) * P]
                    pl.op("tensor", lambda e, l_=lhsT: e.ldweights(l_),
                          waits=[("A", tb["a_eevac"][0])] if (gi == 0 and ncl == 0) else [])
                    for h in range(2):
                        rhs = e_pad[:, h * 512:(h + 1) * 512]
                        dst = ps4[gi % 2][ncl][:, h * 512:(h + 1) * 512]
                        last = (ncl == 1 and h == 1)
                        pl.op("tensor", lambda e, d_=dst, l_=lhsT, r_=rhs:
                              e.matmul(d_, l_, r_, start=False, stop=True, skip_group_check=True),
                              inc="Pe" if last else None)
                pstop6[gi] = pl.val("Pe")
            tb["pstop6"] = pstop6
            return tb

        tb = plan_main(_Plan(dict(ctr)), None)
        pl = _Plan(ctr)
        plan_main(pl, tb)
        _emit_block(nc, pl, sems, "main")

    dump("dbg_w01", W01)
    dump("dbg_patk", patk[:])
    dump("dbg_HT", HT[:])
    dump("dbg_x2", xa[:])
    dump("dbg_x2l", xb[:])
    dump("dbg_e", e_pad[:])

    _CACHE["nc"] = nc
    return nc


def _sw(a, width):
    return np.ascontiguousarray(a.reshape(KC, P, width).transpose(1, 0, 2).reshape(P, KC * width))


def _prep_shared(w, mask, wb, b, enc0, enc1, c0, c1, m0, m1):
    sh = {}
    wT = np.asarray(w, np.float32).T
    sh["wT"] = np.ascontiguousarray(
        wT.reshape(KC, P, NG, 256).transpose(1, 2, 0, 3).reshape(P, NG * 8192))
    wbT = np.asarray(wb, np.float32).T
    sh["wbT"] = np.ascontiguousarray(
        wbT.reshape(KC, P, NG, 256).transpose(1, 2, 0, 3).reshape(P, NG * 8192))
    sh["maskT_full"] = np.ascontiguousarray(np.asarray(mask, np.int32).T)
    sh["FT_full"] = _rfo_matrix_T()
    sh["b2d"] = np.ascontiguousarray(np.asarray(b, np.float32).reshape(KC, P).T)
    encT = np.concatenate([np.asarray(enc0, np.float32), np.asarray(enc1, np.float32)], axis=0).T
    sh["encT"] = _sw(encT, 40)
    sh["c0T"] = _sw(np.asarray(c0, np.float32).transpose(1, 0, 2).reshape(D, 96), 96)
    sh["c1T"] = _sw(np.asarray(c1, np.float32).transpose(1, 0, 2).reshape(D, 24), 24)
    sh["m0T"] = _sw(np.ascontiguousarray(np.asarray(m0).transpose(1, 0, 2).reshape(D, 96)).view(np.uint8), 96)
    sh["m1T"] = _sw(np.ascontiguousarray(np.asarray(m1).transpose(1, 0, 2).reshape(D, 24)).view(np.uint8), 24)
    sh["g32"] = np.ascontiguousarray(
        np.kron(np.eye(4, dtype=np.float32), np.ones((32, 32), np.float32)).astype(BF))
    return sh


def kernel(L0_attn_data, L0_attn_weight, L0_attn_mask, L0_ffn_W_base, L0_ffn_b_base,
           L0_ffn_enc0, L0_ffn_enc1, L0_ffn_coefs0, L0_ffn_coefs1, L0_ffn_mask0, L0_ffn_mask1):
    x = np.asarray(L0_attn_data, np.float32).reshape(NCORES * T, D)
    sh = _prep_shared(L0_attn_weight, L0_attn_mask, L0_ffn_W_base, L0_ffn_b_base,
                      L0_ffn_enc0, L0_ffn_enc1, L0_ffn_coefs0, L0_ffn_coefs1,
                      L0_ffn_mask0, L0_ffn_mask1)
    maskT_full = sh.pop("maskT_full")
    FT_full = sh.pop("FT_full")
    nc = _build_nc()
    in_maps = []
    for i in range(NCORES):
        m = dict(sh)
        xt = x[i * T:(i + 1) * T, :].T
        m["xT"] = _sw(xt, T)
        ms = maskT_full[i * 512:(i + 1) * 512, :]
        m["maskS"] = np.ascontiguousarray(ms.reshape(4, P, D).transpose(1, 0, 2).reshape(P, 4 * D))
        fs = FT_full[:, i * 512:(i + 1) * 512]
        m["FTS"] = _sw(fs, 512)
        in_maps.append(m)
    res = run_bass_kernel_spmd(nc, in_maps, core_ids=list(range(NCORES)))
    out = np.empty((NCORES * T, D), np.float32)
    for i in range(NCORES):
        out[i * T:(i + 1) * T, :] = res.results[i]["out"].T
    return out.reshape(4, 2048, D)
